# version 02
# baseline (speedup 1.0000x reference)
import contextlib
import numpy as np
import ml_dtypes
import concourse.bass as bass
import concourse.mybir as mybir
from concourse.bass_utils import run_bass_kernel_spmd

F32 = mybir.dt.float32
BF16 = mybir.dt.bfloat16
I32 = mybir.dt.int32
AF = mybir.ActivationFunctionType
ALU = mybir.AluOpType
AX = mybir.AxisListType

LN2 = float(np.log(2.0))
INV_LN2 = float(1.0 / np.log(2.0))

N_CORES = 8
B_TOT = 32
T_FULL = 2048
C = 1024
KC = C // 128
F4 = 4096
KF = F4 // 128
D_IN = 2048
KD = D_IN // 128
NST = 16
NEXP = 16
I_MOE = 512
KJ = (2 * I_MOE) // 128
KJ2 = I_MOE // 128
TT = 256
TILE_STOP = 99
EPS = 1e-5
FP8_MAX = 448.0


class Prog:
    ENGS = ("te", "ve", "se", "ge", "sy")

    def __init__(self, nc, ctx):
        self.nc = nc
        self.ctx = ctx
        self.eng = {"te": nc.tensor, "ve": nc.vector, "se": nc.scalar,
                    "ge": nc.gpsimd, "sy": nc.sync}
        self.trk = {}
        self.waited = {}
        for e in self.ENGS:
            self.mktrk(e)

    def mktrk(self, name):
        sem = self.ctx.enter_context(self.nc.semaphore(name))
        self.trk[name] = [sem, 0]
        return name

    def inc(self, name, inst, n=1):
        t = self.trk[name]
        inst.then_inc(t[0], n)
        t[1] += n
        return (name, t[1])

    def bump(self, ename):
        t = self.trk[ename]
        self.eng[ename].sem_inc(t[0], 1)
        t[1] += 1
        return (ename, t[1])

    def dep(self, name):
        return (name, self.trk[name][1])

    def wait(self, ename, *deps):
        for d in deps:
            if d is None:
                continue
            tname, cnt = d
            key = (ename, tname)
            if self.waited.get(key, 0) >= cnt:
                continue
            self.eng[ename].wait_ge(self.trk[tname][0], cnt)
            self.waited[key] = cnt

    def note_emit(self, ename):
        pass

    def barrier(self, engines=("te", "ve", "se", "ge")):
        deps = [self.bump(e) for e in engines]
        for e in engines:
            self.wait(e, *deps)
        return deps


class DmaRing:
    def __init__(self, prog, qeng, name, depth=4):
        self.p = prog
        self.qeng = qeng
        self.names = [prog.mktrk(f"{name}{i}") for i in range(depth)]
        self.i = 0

    def start(self, out, in_, deps=()):
        n = self.names[self.i % len(self.names)]
        self.i += 1
        self.p.wait(self.qeng, self.p.dep(n), *deps)
        inst = self.p.eng[self.qeng].dma_start(out=out, in_=in_)
        return self.p.inc(n, inst, 16)


def _f(x):
    return np.ascontiguousarray(np.asarray(x), dtype=np.float32)


def _t(x):
    return np.ascontiguousarray(np.asarray(x).T, dtype=np.float32)


def prep_host(inputs):
    x = np.asarray(inputs["L0_attn_x"], dtype=np.float32)
    g = {}
    g["wk"] = _t(inputs["L0_attn_att_Wk"])
    g["wv"] = _t(inputs["L0_attn_att_Wv"])
    g["wr"] = _t(inputs["L0_attn_att_Wr"])
    g["wo"] = _t(inputs["L0_attn_att_Wo"])
    g["wshort"] = _t(inputs["L0_attn_short_W"])
    g["wffnk"] = _t(inputs["L0_attn_ffn_Wk"])
    g["wffnr"] = _t(inputs["L0_attn_ffn_Wr"])
    g["wffnv"] = _t(inputs["L0_attn_ffn_Wv"])
    g["wl0"] = _t(inputs["L0_ffn_weight"])
    g["win"] = _f(inputs["L1_attn_W_in"])
    g["wxproj"] = _f(inputs["L1_attn_W_xproj"])
    g["wout"] = _f(inputs["L1_attn_W_out"])
    g["wgate"] = _t(inputs["L1_ffn_gate_w"])
    g["w13"] = np.ascontiguousarray(
        np.asarray(inputs["L1_ffn_w13"]).transpose(0, 2, 1), np.float32)
    g["w2"] = np.ascontiguousarray(
        np.asarray(inputs["L1_ffn_w2"]).transpose(0, 2, 1), np.float32)
    g["w13s"] = _t(inputs["L1_ffn_w13_s"])
    g["w2s"] = _t(inputs["L1_ffn_w2_s"])
    for nm, key in [("ln1w", "L0_attn_ln1_w"), ("ln1b", "L0_attn_ln1_b"),
                    ("ln2w", "L0_attn_ln2_w"), ("ln2b", "L0_attn_ln2_b"),
                    ("tdec", "L0_attn_att_time_decay"),
                    ("tfirst", "L0_attn_att_time_first"),
                    ("mixk", "L0_attn_att_mix_k"), ("mixv", "L0_attn_att_mix_v"),
                    ("mixr", "L0_attn_att_mix_r"),
                    ("fmixk", "L0_attn_ffn_mix_k"), ("fmixr", "L0_attn_ffn_mix_r"),
                    ("l0bias", "L0_ffn_bias")]:
        g[nm] = _f(np.asarray(inputs[key]).reshape(KC, 128))
    g["convw"] = _f(np.asarray(inputs["L1_attn_conv_w"]).reshape(KD, 128, 4))
    g["convb"] = _f(np.asarray(inputs["L1_attn_conv_b"]).reshape(KD, 128))
    g["dvec"] = _f(np.asarray(inputs["L1_attn_D"]).reshape(KD, 128))
    g["gateb"] = _f(np.asarray(inputs["L1_ffn_gate_b"]).reshape(NEXP, 1))
    g["ident_bf"] = np.eye(128, dtype=ml_dtypes.bfloat16)
    g["ident_f"] = np.eye(128, dtype=np.float32)
    g["ones_row_f"] = np.ones((1, 128), np.float32)
    g["ones_col_bf"] = np.ones((128, 1), ml_dtypes.bfloat16)
    x_tm = x.reshape(-1, C)
    x_fm = np.ascontiguousarray(x.transpose(0, 2, 1))
    bc = x.shape[0] // N_CORES
    t = x.shape[1]
    percore = []
    for c in range(N_CORES):
        own = slice(c * bc * t, (c + 1) * bc * t)
        percore.append({
            "x_fm": np.ascontiguousarray(x_fm[c * bc:(c + 1) * bc]),
            "x_tm": np.ascontiguousarray(
                np.concatenate([x_tm[own], x_tm[:own.start],
                                x_tm[own.stop:]], axis=0)),
        })
    return g, percore


class Build:
    def __init__(self, mode, bc, t_full, n_stats_rows):
        self.mode = mode
        self.bc = bc
        self.T = t_full
        self.NTT = t_full // TT
        self.nst = n_stats_rows
        self.own_tile0 = 0
        self.nc = bass.Bass("TRN2", target_bir_lowering=False)
        self.ctx = contextlib.ExitStack()
        self.p = Prog(self.nc, self.ctx)
        self.io = {}
        self.sb = {}

    def declare_io(self):
        nc, g = self.nc, self.io
        bc, t = self.bc, self.T

        def din(name, shape, dt=F32):
            g[name] = nc.dram_tensor(name, list(shape), dt,
                                     kind="ExternalInput").ap()

        def dout(name, shape, dt=F32):
            g[name] = nc.dram_tensor(name, list(shape), dt,
                                     kind="ExternalOutput").ap()

        def dint(name, shape, dt=F32):
            g[name] = nc.dram_tensor(name, list(shape), dt,
                                     kind="Internal").ap()

        din("ident_bf", (128, 128), BF16)
        din("ident_f", (128, 128), F32)
        din("ones_row_f", (1, 128), F32)
        din("ones_col_bf", (128, 1), BF16)
        if self.mode in ("l0", "fused"):
            din("x_fm", (bc, C, t))
            din("x_tm", (self.nst, C))
            for nm, sh in [("wk", (C, C)), ("wv", (C, C)), ("wr", (C, C)),
                           ("wo", (C, C)), ("wshort", (C, C)),
                           ("wffnk", (C, F4)), ("wffnr", (C, C)),
                           ("wffnv", (F4, C)), ("wl0", (C, C))]:
                din(nm, sh)
            for nm in ["ln1w", "ln1b", "ln2w", "ln2b", "tdec", "tfirst",
                       "mixk", "mixv", "mixr", "fmixk", "fmixr", "l0bias"]:
                din(nm, (KC, 128))
            dint("strows", (4, bc * t))
            for nm, sh in [("wkB", (C, C)), ("wvB", (C, C)), ("wrB", (C, C)),
                           ("woB", (C, C)), ("wshortB", (C, C)),
                           ("wffnkB", (C, F4)), ("wffnrB", (C, C)),
                           ("wffnvB", (F4, C)), ("wl0B", (C, C))]:
                dint(nm, sh, BF16)
        if self.mode in ("l1", "fused"):
            din("win", (C, F4))
            din("wxproj", (D_IN, 33))
            din("wout", (D_IN, C))
            din("wgate", (C, NEXP))
            din("w13", (NEXP, C, 2 * I_MOE))
            din("w2", (NEXP, I_MOE, C))
            din("w13s", (C, 2 * I_MOE))
            din("w2s", (I_MOE, C))
            din("convw", (KD, 128, 4))
            din("convb", (KD, 128))
            din("dvec", (KD, 128))
            din("gateb", (NEXP, 1))
            for nm, sh in [("winB", (C, F4)), ("woutB", (D_IN, C)),
                           ("w13B", (NEXP, C, 2 * I_MOE)),
                           ("w2B", (NEXP, I_MOE, C)),
                           ("w13sB", (C, 2 * I_MOE)), ("w2sB", (I_MOE, C)),
                           ("wxprojB", (D_IN, 33)), ("wgateB", (C, NEXP))]:
                dint(nm, sh, BF16)
            dint("xcSp", (bc, D_IN, t), BF16)
            dint("silzSp", (bc, D_IN, t), BF16)
            dint("xn1Sp", (bc, C, t), BF16)
            dint("s1Sp", (bc, C, t), BF16)
            dint("combSp", (bc, NEXP, t))
        if self.mode == "l0":
            dout("x1", (bc, C, t))
        elif self.mode == "l1":
            din("x1", (bc, C, t))
            dout("out_fm", (bc, C, t))
        else:
            dint("x1", (bc, C, t))
            dout("out_fm", (bc, C, t))
        return g

    def alloc(self):
        E = self.ctx.enter_context
        nc, s = self.nc, self.sb

        def sb(name, shape, dt=F32):
            s[name] = E(nc.sbuf_tensor("S_" + name, list(shape), dt))
            return s[name]

        l0 = self.mode in ("l0", "fused")
        l1 = self.mode in ("l1", "fused")
        sb("ident_bf", (128, 128), BF16)
        sb("ident_f", (128, 128), F32)
        sb("ones_f", (1, 128), F32)
        sb("onec_bf", (128, 1), BF16)
        sb("scal", (128, 8))
        sb("kc", (128, 4))
        sb("rowA", (1, TT))
        sb("rowB", (1, TT))
        sb("rowC", (1, TT))
        sb("rowD", (1, TT))
        nf = 8 if l0 else 4
        for i in range(nf):
            sb(f"f{i}", (128, KC, TT))
        nb = 12 if l0 else 6
        for i in range(nb):
            sb(f"b{i}", (128, KC, TT), BF16)
        if l0:
            sb("kkbf", (128, KF, TT), BF16)
            sb("wkv", (128, 10, TT))
            sb("carry", (128, 2 * KC))
            sb("hcar", (128, KC))
            sb("gcar", (128, KC))
            sb("qi", (128, 1024), I32)
            sb("wdec", (128, KC))
            sb("wdecT", (128, KC * TT))
            for nm in ["ln1w", "ln1b", "ln2w", "ln2b", "tdec", "tfirst",
                       "mixk", "mixv", "mixr", "fmixk", "fmixr", "l0bias"]:
                sb(nm, (128, KC))
            sb("stin", (128, 2, C))
            sb("stsum", (128, 512))
            sb("stssq", (128, 512))
            sb("stmax", (128, 512))
            sb("stmin", (128, 512))
            sb("sttmp", (128, 512))
            sb("sttr", (64, 128))
            sb("stamx", (128, 1))
            sb("sttrash", (128, C))
            sb("sttrash2", (128, C))
        sb("ws0", (128, 2, KC, 128), BF16)
        sb("ws1", (128, 2, KC, 128), BF16)
        sb("ws2", (128, 1, KF, 128), BF16)
        if l1:
            sb("convw", (128, KD, 4))
            sb("convb", (128, KD))
            sb("dvec", (128, KD))
            sb("gatebS", (NEXP, 1))
            sb("wxpS", (128, KD, 33), BF16)
            sb("wgateS", (128, KC, NEXP), BF16)
            sb("xph", (128, KD, TT + 3), BF16)
            sb("xcb", (128, KD, TT), BF16)
            sb("silz", (128, KD, TT), BF16)
            sb("ssmE", (33, TT))
            sb("ssmT", (128, 2 * 33))
            sb("dBtm", (128, 2, NST), BF16)
            sb("cmT", (NST, self.T), BF16)
            sb("hmT", (NST, D_IN), BF16)
            sb("hm", (128, KD * NST))
            sb("spc", (128, 8))
            sb("scE", (NEXP, TT))
            sb("biE", (NEXP, TT))
            sb("combS", (NEXP, 128))
            sb("scrT", (128, NEXP))
            sb("biT", (128, NEXP))
            sb("biT0", (128, NEXP))
            sb("mskT", (128, NEXP))
            sb("xcl", (128, D_IN), BF16)
            sb("moeW13", (128, 2, KC, 2 * I_MOE), BF16)
            sb("moeW2", (128, 2, KJ2, C), BF16)
            sb("yacc", (128, KC, self.T), BF16)
            sb("xn1m", (128, KC, TT), BF16)
            sb("cbt", (128, TT))
            sb("silug", (128, TT), BF16)
            sb("hp", (128, KJ2, TT), BF16)
        s["psA"] = E(nc.psum_tensor("psA", [128, 3584], F32))
        s["psB"] = E(nc.psum_tensor("psB", [128, 512], BF16))
        return s

    def pb(self, bank, width=None, part=128):
        w = 512 if width is None else width
        return self.sb["psA"][0:part, bank * 512:bank * 512 + w]

    def pbx(self, bank, off, width, part=128):
        return self.sb["psA"][0:part, bank * 512 + off:bank * 512 + off + width]

    def finish(self):
        self.ctx.close()
        return self.nc

    def op(self, eng, method, *args, deps=(), inc=True, **kw):
        deps = list(deps)
        if eng in ("ve", "se", "ge"):
            deps.append(self.p.dep(eng))
        self.p.wait(eng, *deps)
        inst = getattr(self.p.eng[eng], method)(*args, **kw)
        return self.p.inc(eng, inst) if inc else None

    def emit_startup(self):
        g, s, p = self.io, self.sb, self.p
        self.dC = DmaRing(p, "sy", "dC", 4)
        self.dW = DmaRing(p, "sy", "dW", 6)
        self.dX = DmaRing(p, "sy", "dX", 4)
        self.dR = DmaRing(p, "sy", "dR", 4)
        self.dS = DmaRing(p, "ge", "dS", 4)
        self.dO = DmaRing(p, "sy", "dO", 4)
        d = []
        d.append(self.dC.start(s["ident_bf"][:], g["ident_bf"]))
        d.append(self.dC.start(s["ident_f"][:], g["ident_f"]))
        d.append(self.dC.start(s["ones_f"][:], g["ones_row_f"]))
        d.append(self.dC.start(s["onec_bf"][:], g["ones_col_bf"]))
        casts = []
        if self.mode in ("l0", "fused"):
            for nm in ["ln1w", "ln1b", "ln2w", "ln2b", "tdec", "tfirst",
                       "mixk", "mixv", "mixr", "fmixk", "fmixr", "l0bias"]:
                for k in range(KC):
                    d.append(self.dC.start(s[nm][:, k:k + 1], g[nm][k:k + 1, :].rearrange("a b -> b a")))
            for src, dst in [("wk", "wkB"), ("wv", "wvB"), ("wr", "wrB"),
                             ("wo", "woB"), ("wshort", "wshortB"),
                             ("wffnk", "wffnkB"), ("wffnr", "wffnrB"),
                             ("wffnv", "wffnvB")]:
                casts.append(self.dS.start(g[dst], g[src]))
        if self.mode in ("l1", "fused"):
            for nm in ["convb", "dvec"]:
                for k in range(KD):
                    d.append(self.dC.start(s[nm][:, k:k + 1], g[nm][k:k + 1, :].rearrange("a b -> b a")))
            for k in range(KD):
                d.append(self.dC.start(s["convw"][:, k, :], g["convw"][k]))
            d.append(self.dC.start(s["gatebS"][:], g["gateb"]))
            for src, dst in [("win", "winB"), ("wout", "woutB"),
                             ("w13", "w13B"), ("w2", "w2B"),
                             ("w13s", "w13sB"), ("w2s", "w2sB"),
                             ("wxproj", "wxprojB"), ("wgate", "wgateB")]:
                casts.append(self.dS.start(g[dst], g[src]))
            self.wxp_dep = self.dW.start(
                s["wxpS"][:], g["wxprojB"].rearrange("(k p) o -> p k o", p=128),
                deps=casts[-2:])
            self.wgate_dep = self.dW.start(
                s["wgateS"][:], g["wgateB"].rearrange("(k p) o -> p k o", p=128),
                deps=casts[-1:])
        kd = []
        for col, val in [(0, 1e-30), (1, EPS), (2, 3 * LN2), (3, -3 * LN2)]:
            kd.append(self.op("ve", "memset", s["kc"][:, col:col + 1], val))
        self.kc_dep = kd[-1]
        self.const_dep = d + kd
        self.const_all = tuple(self.p.dep(n) for n in self.dC.names) + (kd[-1],)
        self.cast_dep = casts
        if self.mode in ("l0", "fused"):
            cd = []
            for nm in ["mixk", "mixv", "mixr", "fmixk", "fmixr"]:
                t = self.ctx.enter_context(
                    self.nc.sbuf_tensor(nm + "1", [128, KC], F32))
                self.sb[nm + "1"] = t
                cd.append(self.op("ve", "tensor_scalar", t[:], s[nm][:],
                                  -1.0, 1.0, ALU.mult, ALU.add, deps=d))
            cd.append(self.op("se", "activation", s["wdec"][:], s["tdec"][:],
                              AF.Exp, deps=d))
            cd.append(self.op("se", "activation", s["wdec"][:], s["wdec"][:],
                              AF.Exp, scale=-1.0, deps=cd[-1:]))
            cd.append(self.op("ve", "memset", s["wdecT"][:], 0.0))
            for k in range(KC):
                cd.append(self.op("ve", "tensor_scalar_add",
                                  s["wdecT"][:, k * TT:(k + 1) * TT],
                                  s["wdecT"][:, k * TT:(k + 1) * TT],
                                  s["wdec"][:, k:k + 1], deps=cd[-2:]))
            self.cvec_dep = cd

    def emit_stats(self):
        g, s, p = self.io, self.sb, self.p
        ntile = self.nst // 128
        self.st_consumed = {0: None, 1: None}
        self.st_consumed_se = None
        vdeps = []
        sdeps = []
        for i in range(ntile):
            par = i % 2
            ld2 = self.dX.start(s["stin"][:, par, :],
                                g["x_tm"][i * 128:(i + 1) * 128, :],
                                deps=([self.st_consumed[par],
                                       self.st_consumed_se] if i >= 2
                                      else [p.dep("ve"), p.dep("se")]))
            xin = s["stin"][:, par, :]
            ci = i % 512
            d1 = self.op("ve", "tensor_scalar", s["sttrash"][:],
                         xin, 0.0, None, ALU.add, ALU.max,
                         accum_out=s["stmax"][:, ci:ci + 1], deps=[ld2])
            d2 = self.op("ve", "tensor_scalar", s["sttrash"][:],
                         xin, 0.0, None, ALU.add, ALU.min,
                         accum_out=s["stmin"][:, ci:ci + 1], deps=[ld2, d1])
            d3 = self.op("se", "activation", s["sttrash2"][:], xin,
                         AF.Identity, accum_out=s["stsum"][:, ci:ci + 1],
                         deps=[ld2])
            d4 = self.op("se", "activation", s["sttrash2"][:], xin,
                         AF.Square, accum_out=s["stssq"][:, ci:ci + 1],
                         deps=[ld2, d3])
            self.st_consumed[par] = d2
            self.st_consumed_se = d4
            vdeps = [d1, d2]
            sdeps = [d3, d4]
            if ci == 511 or i == ntile - 1:
                n = ci + 1
                fin = self._stats_final(i - ci, n, vdeps + sdeps)
                vdeps = [fin]
                sdeps = []
        dmax = self.op("ve", "reduce_max", s["sttmp"][:, 0:1],
                       s["stamx"][:, 0:1], AX.X, deps=vdeps)
        dtr = self.op("te", "transpose", self.pb(6, 128, part=1),
                      s["sttmp"][:, 0:1], s["ident_f"][:], deps=[dmax])
        dmx = self.op("ve", "reduce_max", s["scal"][0:1, 6:7],
                      self.pb(6, 128, part=1), AX.X, deps=[dtr])
        dmx = self.op("ve", "tensor_scalar_max", s["scal"][0:1, 6:7],
                      s["scal"][0:1, 6:7], 1e-12, deps=[dmx])
        dmx = self.op("ve", "reciprocal", s["scal"][0:1, 3:4],
                      s["scal"][0:1, 6:7], deps=[dmx])
        dmx = self.op("ve", "tensor_scalar_mul", s["scal"][0:1, 3:4],
                      s["scal"][0:1, 3:4], FP8_MAX, deps=[dmx])
        self.sx_dep = dmx
        self.rows_dep_all = tuple(p.dep(n) for n in self.dS.names)
        return dmx

    def _stats_final(self, tile0, n, deps):
        g, s, p = self.io, self.sb, self.p
        cs = slice(0, n)
        d0 = self.op("ve", "tensor_scalar_mul", s["stsum"][:, cs],
                     s["stsum"][:, cs], 1.0 / C, deps=deps)
        d1 = self.op("ve", "tensor_tensor", s["sttmp"][:, cs],
                     s["stsum"][:, cs], s["stsum"][:, cs], ALU.mult,
                     deps=[d0])
        d2 = self.op("ve", "tensor_scalar", s["stssq"][:, cs],
                     s["stssq"][:, cs], 1.0 / C, None, ALU.mult, deps=deps)
        d3 = self.op("ve", "tensor_tensor", s["stssq"][:, cs],
                     s["stssq"][:, cs], s["sttmp"][:, cs], ALU.subtract,
                     deps=[d1, d2])
        d4 = self.op("se", "activation", s["sttmp"][:, cs], s["stssq"][:, cs],
                     AF.Ln, bias=s["kc"][:, 1:2], deps=[d3, self.kc_dep])
        d5 = self.op("se", "activation", s["sttmp"][:, cs], s["sttmp"][:, cs],
                     AF.Exp, scale=-0.5, deps=[d4])
        d6 = self.op("ve", "tensor_tensor", s["stmax"][:, cs],
                     s["stmax"][:, cs], s["stsum"][:, cs], ALU.subtract,
                     deps=[d0] + deps)
        d7 = self.op("ve", "tensor_tensor", s["stmin"][:, cs],
                     s["stsum"][:, cs], s["stmin"][:, cs], ALU.subtract,
                     deps=[d0] + deps)
        d8 = self.op("ve", "tensor_tensor", s["stmax"][:, cs],
                     s["stmax"][:, cs], s["stmin"][:, cs], ALU.max, deps=[d6, d7])
        d9 = self.op("ve", "tensor_tensor", s["stmax"][:, cs],
                     s["stmax"][:, cs], s["sttmp"][:, cs], ALU.mult,
                     deps=[d5, d8])
        if tile0 == 0:
            d10 = self.op("ve", "reduce_max", s["stamx"][:, 0:1],
                          s["stmax"][:, cs], AX.X, deps=[d9])
        else:
            d10 = self.op("ve", "reduce_max", s["stamx"][:, 1:2],
                          s["stmax"][:, cs], AX.X, deps=[d9])
            d10 = self.op("ve", "tensor_tensor", s["stamx"][:, 0:1],
                          s["stamx"][:, 0:1], s["stamx"][:, 1:2], ALU.max,
                          deps=[d10])
        own0, nown = self.own_tile0, (self.bc * self.T) // 128
        lo = max(own0, tile0)
        hi = min(own0 + nown, tile0 + n)
        dret = d10
        if lo < hi:
            cs2 = slice(lo - tile0, hi - tile0)
            dA = self.op("ve", "tensor_tensor", s["stmin"][:, cs2],
                         s["sttmp"][:, cs2], s["sttmp"][:, cs2], ALU.mult,
                         deps=[d9])
            dA = self.op("ve", "tensor_tensor", s["stmin"][:, cs2],
                         s["stssq"][:, cs2], s["stmin"][:, cs2], ALU.mult,
                         deps=[dA])
            dB = self.op("se", "activation", s["stmin"][:, cs2],
                         s["stmin"][:, cs2], AF.Ln, bias=s["kc"][:, 1:2],
                         deps=[dA])
            dB = self.op("se", "activation", s["stmin"][:, cs2],
                         s["stmin"][:, cs2], AF.Exp, scale=-0.5, deps=[dB])
            dC = self.op("ve", "tensor_tensor", s["stssq"][:, cs2],
                         s["stsum"][:, cs2], s["sttmp"][:, cs2], ALU.mult,
                         deps=[d5, d9])
            last_cp = None
            last_dma = None
            for blk0 in range(lo, hi, 64):
                blkn = min(64, hi - blk0)
                bs = slice(blk0 - tile0, blk0 - tile0 + blkn)
                row0 = (blk0 - own0) * 128
                for srcnm, ridx, dd in [("stsum", 0, [d9]),
                                        ("sttmp", 1, [d9, dC]),
                                        ("stmin", 2, [dB]), ("stssq", 3, [dC])]:
                    dt = self.op("te", "transpose", self.pb(6, 128, part=blkn),
                                 s[srcnm][:, bs], s["ident_f"][:],
                                 deps=dd + [last_cp])
                    last_cp = self.op("ve", "tensor_copy", s["sttr"][0:blkn, :],
                                      self.pb(6, 128, part=blkn),
                                      deps=[dt, last_dma])
                    last_dma = self.dS.start(
                        g["strows"][ridx:ridx + 1,
                                    row0:row0 + blkn * 128].rearrange(
                                        "a (p q) -> (a p) q", p=blkn),
                        s["sttr"][0:blkn, :], deps=[last_cp])
            self.rows_dep = last_dma
            dret = (d10 if True else None)
        return dret

    def emit_quant_chunk(self, src, dst_bf, qi, w, scol, deps):
        o = self.op
        kc = self.sb["kc"]
        d0 = o("ve", "tensor_scalar_mul", w["v"], src, scol, deps=deps)
        d1 = o("se", "activation", w["a"], w["v"], AF.Abs, deps=[d0])
        d1 = o("se", "activation", w["a"], w["a"], AF.Ln, bias=kc[:, 0:1],
               deps=[d1, self.kc_dep])
        d2 = o("ve", "tensor_scalar_mul", w["e"], w["a"], INV_LN2, deps=[d1])
        d2 = o("ve", "tensor_copy", qi, w["e"], deps=[d2])
        d2 = o("ve", "tensor_copy", w["n"], qi, deps=[d2])
        d2 = o("ve", "tensor_tensor", w["c"], w["n"], w["e"], ALU.is_gt,
               deps=[d2])
        d2 = o("ve", "tensor_tensor", w["e"], w["n"], w["c"], ALU.subtract,
               deps=[d2])
        d2 = o("ve", "tensor_scalar_max", w["e"], w["e"], -6.0, deps=[d2])
        d3 = o("se", "activation", w["a"], w["e"], AF.Exp, scale=-LN2,
               bias=kc[:, 2:3], deps=[d2])
        d4 = o("ve", "tensor_tensor", w["n"], w["v"], w["a"], ALU.mult,
               deps=[d3])
        d5 = o("se", "activation", w["a"], w["e"], AF.Exp, scale=LN2,
               bias=kc[:, 3:4], deps=[d4])
        d6 = o("ve", "tensor_copy", qi, w["n"], deps=[d4])
        d6 = o("ve", "tensor_copy", w["f"], qi, deps=[d6])
        d7 = o("ve", "tensor_tensor", w["e"], w["n"], w["f"], ALU.subtract,
               deps=[d6, d5])
        d8 = o("ve", "tensor_single_scalar", w["c"], w["e"], 0.5, ALU.is_gt,
               deps=[d7])
        d8 = o("ve", "tensor_tensor", w["f"], w["f"], w["c"], ALU.add,
               deps=[d8])
        d9 = o("ve", "tensor_single_scalar", w["c"], w["e"], -0.5, ALU.is_lt,
               deps=[d8])
        d9 = o("ve", "tensor_tensor", w["f"], w["f"], w["c"], ALU.subtract,
               deps=[d9])
        da = o("ve", "tensor_tensor", dst_bf, w["f"], w["a"], ALU.mult,
               deps=[d9, d5])
        return da

    def emit_wquant(self):
        g, s, p, o = self.io, self.sb, self.p, self.op
        self.io["wl0Q"] = self.nc.dram_tensor(
            "wl0Q", [C, C], BF16, kind="Internal").ap()
        wsrc = g["wl0"].rearrange("(n p) o -> n p o", p=128)
        wdst = g["wl0Q"].rearrange("(n p) o -> n p o", p=128)
        tin = s["stin"]
        acc = []
        for i in range(KC):
            ld = self.dX.start(tin[:, i % 2, :], wsrc[i],
                               deps=[acc[-2] if i >= 2 else None])
            d1 = o("ve", "tensor_scalar", s["sttrash"][:], tin[:, i % 2, :],
                   0.0, None, ALU.add, ALU.max,
                   accum_out=s["stmax"][:, i:i + 1], deps=[ld])
            d2 = o("ve", "tensor_scalar", s["sttrash"][:], tin[:, i % 2, :],
                   0.0, None, ALU.add, ALU.min,
                   accum_out=s["stmin"][:, i:i + 1], deps=[ld, d1])
            acc.append(d2)
        d = o("ve", "tensor_scalar_mul", s["stmin"][:, 0:KC],
              s["stmin"][:, 0:KC], -1.0, deps=acc)
        d = o("ve", "tensor_tensor", s["stmax"][:, 0:KC], s["stmax"][:, 0:KC],
              s["stmin"][:, 0:KC], ALU.max, deps=[d])
        d = o("ve", "reduce_max", s["stamx"][:, 0:1], s["stmax"][:, 0:KC],
              AX.X, deps=[d])
        dt = o("te", "transpose", self.pb(6, 128, part=1), s["stamx"][:, 0:1],
               s["ident_f"][:], deps=[d])
        d = o("ve", "reduce_max", s["scal"][0:1, 4:5],
              self.pb(6, 128, part=1), AX.X, deps=[dt])
        d = o("ve", "tensor_scalar_max", s["scal"][0:1, 4:5],
              s["scal"][0:1, 4:5], 1e-12, deps=[d])
        d = o("ve", "reciprocal", s["scal"][0:1, 5:6], s["scal"][0:1, 4:5],
              deps=[d])
        d = o("ve", "tensor_scalar_mul", s["scal"][0:1, 5:6],
              s["scal"][0:1, 5:6], FP8_MAX, deps=[d])
        db = o("te", "matmul", self.pb(6, 1), s["ones_f"][:],
               s["scal"][0:1, 5:6], start=True, stop=True, deps=[d, dt])
        d = o("ve", "tensor_copy", s["scal"][:, 2:3], self.pb(6, 1), deps=[db])
        self.sw_dep = d
        wlanes = {"v": s["wkv"][:, 0, :C][:, 0:C] if False else None}
        wk = s["wkv"]
        lanes = {"v": wk[:, 0:4, :].rearrange("p a b -> p (a b)"),
                 "a": wk[:, 4:8, :].rearrange("p a b -> p (a b)"),
                 "e": s["f6"][:, 0:4, :].rearrange("p a b -> p (a b)"),
                 "n": s["f6"][:, 4:8, :].rearrange("p a b -> p (a b)"),
                 "f": s["f7"][:, 0:4, :].rearrange("p a b -> p (a b)"),
                 "c": s["f7"][:, 4:8, :].rearrange("p a b -> p (a b)")}
        qiv = s["qi"][:, 0:1024]
        dst = s["b0"][:, :, :].rearrange("p a b -> p (a b)")[:, 0:C]
        st = None
        for i in range(KC):
            ld = self.dX.start(tin[:, i % 2, :], wsrc[i],
                               deps=[st, acc[-1]])
            dq = self.emit_quant_chunk(tin[:, i % 2, :], dst, qiv, lanes,
                                       s["scal"][:, 2:3], [ld, self.sw_dep])
            st = self.dS.start(wdst[i], dst, deps=[dq])
        self.wl0q_dep = st

    def emit_scales(self):
        s, o = self.sb, self.op
        d = o("ve", "tensor_tensor", s["scal"][0:1, 7:8], s["scal"][0:1, 3:4],
              s["scal"][0:1, 5:6], ALU.mult, deps=[self.sx_dep, self.sw_dep])
        d = o("ve", "reciprocal", s["scal"][0:1, 7:8], s["scal"][0:1, 7:8],
              deps=[d])
        db = o("te", "matmul", self.pb(6, 1), s["ones_f"][:],
               s["scal"][0:1, 7:8], start=True, stop=True, deps=[d])
        d0 = o("ve", "tensor_copy", s["scal"][:, 0:1], self.pb(6, 1),
               deps=[db])
        db = o("te", "matmul", self.pb(6, 1), s["ones_f"][:],
               s["scal"][0:1, 3:4], start=True, stop=True, deps=[d0])
        d1 = o("ve", "tensor_copy", s["scal"][:, 1:2], self.pb(6, 1),
               deps=[db])
        self.inv_dep = d0
        self.sx_bcast_dep = d1
        self.bc6_rd = d1

    def emit_l0_batch(self, b):
        g, s, p, o = self.io, self.sb, self.p, self.op
        xin = g["x_fm"].rearrange("b (k p) t -> b p k t", p=128)
        xout = g["x1"].rearrange("b (k p) t -> b p k t", p=128)
        dz = [o("ve", "memset", s["carry"][:], 0.0),
              o("ve", "memset", s["hcar"][:], 0.0),
              o("ve", "memset", s["gcar"][:], 0.0)]
        if not hasattr(self, "x0_rd"):
            self.x0_rd = {0: None, 1: None}
            self.x0_ld = {0: None, 1: None}
            self.ws_rd = {"ws0": {0: None, 1: None}, "ws1": {0: None, 1: None},
                          "ws2": {0: None, 1: None}}
            self.wsi = {"ws0": 0, "ws1": 0, "ws2": 0}
            self.bank_rd = {0: None, 1: None, 2: None, 3: None}
            self.banki = 0
            self.bc4_rd = None
            self.bc6_rd = None
            self.x0_ld[0] = self.dX.start(
                s["f0"][:], xin[0, :, :, 0:TT], deps=())
        for t in range(self.NTT):
            self.emit_l0_tile(b, t, dz if t == 0 else [])

    def _stripe(self, wname, slot, ochunk, nk=KC, owidth=128):
        s, p = self.sb, self.p
        par = (self.wsi[slot] % 2) if slot != "ws2" else 0
        self.wsi[slot] += 1
        w = self.io[wname].rearrange("(k p) o -> p k o", p=128)
        dst = s[slot][:, par, 0:nk, 0:owidth]
        dep = self.dW.start(dst,
                            w[:, 0:nk, ochunk * owidth:(ochunk + 1) * owidth],
                            deps=[self.ws_rd[slot][par]] + list(self.cast_dep)
                            + [getattr(self, "wl0q_dep", None)])
        return par, dep

    def _bank(self):
        bi = self.banki % 4
        self.banki += 1
        return bi, self.bank_rd[bi]

    def emit_l0_tile(self, b, t, extra_deps):
        g, s, p, o = self.io, self.sb, self.p, self.op
        xin = g["x_fm"].rearrange("b (k p) t -> b p k t", p=128)
        xout = g["x1"].rearrange("b (k p) t -> b p k t", p=128)
        par = (b * self.NTT + t) % 2
        x0 = s["f0"] if par == 0 else s["f1"]
        tok0 = t * TT
        row0 = b * self.T + tok0
        xld = self.x0_ld[par]
        nb, nt = (b, t + 1) if t + 1 < self.NTT else (b + 1, 0)
        if nb < self.bc:
            np_ = 1 - par
            self.x0_ld[np_] = self.dX.start(
                (s["f0"] if np_ == 0 else s["f1"])[:],
                xin[nb, :, :, nt * TT:(nt + 1) * TT],
                deps=[self.x0_rd[np_]])
        rdep = list(getattr(self, "rows_dep_all", ()))
        dr1 = self.dR.start(s["rowA"][:], g["strows"][3:4, row0:row0 + TT],
                            deps=rdep + [self.rowA_rd if hasattr(self, "rowA_rd")
                                         else None])
        dr2 = self.dR.start(s["rowB"][:], g["strows"][1:2, row0:row0 + TT],
                            deps=rdep + [self.bc6_rd])
        dr3 = self.dR.start(s["rowC"][:], g["strows"][2:3, row0:row0 + TT],
                            deps=rdep)
        dbc1 = o("te", "matmul", self.pb(4, TT), s["ones_f"][:], s["rowA"][:],
                 start=True, stop=True, deps=[dr1, self.bc4_rd])
        dbc2 = o("te", "matmul", self.sb["psA"][:, 4 * 512 + TT:4 * 512 + 2 * TT],
                 s["ones_f"][:], s["rowB"][:], start=True, stop=True,
                 deps=[dr2])
        dbc3 = o("te", "matmul", self.pb(5, TT), s["ones_f"][:], s["rowC"][:],
                 start=True, stop=True, deps=[dr3, getattr(self, "bc5_rd", None)])
        self.rowA_rd = dbc3
        pbMUR = self.pb(4, TT)
        pbR = self.sb["psA"][:, 4 * 512 + TT:4 * 512 + 2 * TT]
        pbR1 = self.pb(5, TT)
        dxn = []
        dh = []
        for k in range(KC):
            d1 = o("ve", "tensor_tensor", s["f2"][:, k, :], x0[:, k, :], pbR,
                   ALU.mult,
                   deps=[xld, dbc2, getattr(self, "te_tail", None),
                         getattr(self, "f7_st", None)] + extra_deps)
            d1 = o("ve", "tensor_tensor", s["f2"][:, k, :], s["f2"][:, k, :],
                   pbMUR, ALU.subtract, deps=[d1, dbc1])
            dxn.append(d1)
            d2 = o("ve", "tensor_tensor", s["f3"][:, k, :], s["f2"][:, k, :],
                   pbR1, ALU.mult, deps=[d1, dbc3])
            d2 = o("ve", "tensor_scalar", s["f3"][:, k, :], s["f3"][:, k, :],
                   s["ln1w"][:, k:k + 1], s["ln1b"][:, k:k + 1],
                   ALU.mult, ALU.add, deps=[d2])
            dh.append(d2)
            dxb = o("ve", "tensor_copy", s["b0"][:, k, :], s["f2"][:, k, :],
                    deps=[d1])
            dxn[k] = (dxn[k], dxb)
        self.bc4_rd = dxn[-1][0]
        self.bc5_rd = dh[-1]
        if TILE_STOP <= 1:
            return
        lanes = {"v": s["wkv"][:, 0, :], "a": s["wkv"][:, 1, :],
                 "e": s["wkv"][:, 2, :], "n": s["wkv"][:, 3, :],
                 "f": s["wkv"][:, 4, :], "c": s["wkv"][:, 5, :]}
        dxq = []
        for k in range(KC):
            dq = self.emit_quant_chunk(s["f2"][:, k, :], s["b9"][:, k, :],
                                       s["qi"][:, 0:TT], lanes,
                                       s["scal"][:, 1:2],
                                       [dxn[k][0], self.sx_bcast_dep])
            dxq.append(dq)
        dg0 = []
        for oc in range(KC):
            wpar, wd = self._stripe("wl0Q", "ws1", oc)
            bi, brd = self._bank()
            dm = None
            for k in range(KC):
                dm = o("te", "matmul", self.pb(bi, TT),
                       s["ws1"][:, wpar, k, :], s["b9"][:, k, :],
                       start=(k == 0), stop=(k == KC - 1),
                       deps=([wd, brd, dxq[k], self.wl0q_dep] if k == 0
                             else [dxq[k]]), inc=(k == KC - 1))
            self.ws_rd["ws1"][wpar] = dm
            de = o("se", "activation", s["b10"][:, oc, :], self.pb(bi, TT),
                   AF.Silu, bias=s["l0bias"][:, oc:oc + 1],
                   scale=s["scal"][:, 0:1], deps=[dm, self.inv_dep])
            self.bank_rd[bi] = de
            dg0.append(de)
        if TILE_STOP <= 2:
            return
        dmix = {"mk": [], "mv": [], "mr": []}
        for nm, mixn, dst in [("mk", "mixk", "b1"), ("mv", "mixv", "b2"),
                              ("mr", "mixr", "b3")]:
            for k in range(KC):
                dm1 = o("ve", "tensor_scalar_mul", s["f6"][:, k, 1:TT],
                        s["f3"][:, k, 1:TT], s[mixn][:, k:k + 1],
                        deps=[dh[k]])
                dm1 = o("ve", "scalar_tensor_tensor", s[dst][:, k, 1:TT],
                        s["f3"][:, k, 0:TT - 1], s[mixn + "1"][:, k:k + 1],
                        s["f6"][:, k, 1:TT], ALU.mult, ALU.add,
                        deps=[dm1, self.cvec_dep[-1]])
                dm2 = o("ve", "tensor_scalar_mul", s["f6"][:, k, 0:1],
                        s["f3"][:, k, 0:1], s[mixn][:, k:k + 1], deps=[dh[k]])
                dm2 = o("ve", "scalar_tensor_tensor", s[dst][:, k, 0:1],
                        s["hcar"][:, k:k + 1], s[mixn + "1"][:, k:k + 1],
                        s["f6"][:, k, 0:1], ALU.mult, ALU.add, deps=[dm2])
                dmix[nm].append((dm1, dm2))
        dhc = o("ve", "tensor_copy", s["hcar"][:], s["f3"][:, :, TT - 1],
                deps=[d for pr in dmix.values() for dd in pr for d in dd])
        self.hcar_dep = dhc
        if TILE_STOP <= 3:
            return
        projs = [("wkB", "b1", "k"), ("wvB", "b2", "v"), ("wrB", "b3", "r")]
        dk, dv, dsr = [], [], []
        for wname, src, what in projs:
            for oc in range(KC):
                wpar, wd = self._stripe(wname, "ws0", oc)
                bi, brd = self._bank()
                dm = None
                for k in range(KC):
                    mx = dmix[{"k": "mk", "v": "mv", "r": "mr"}[what]][k]
                    dep0 = [wd, brd, mx[0], mx[1]]
                    dm = o("te", "matmul", self.pb(bi, TT),
                           s["ws0"][:, wpar, k, :], s[src][:, k, :],
                           start=(k == 0), stop=(k == KC - 1),
                           deps=dep0 if k == 0 else
                           [dmix[{"k": "mk", "v": "mv", "r": "mr"}[what]][k][0],
                            dmix[{"k": "mk", "v": "mv", "r": "mr"}[what]][k][1]],
                           inc=(k == KC - 1))
                self.ws_rd["ws0"][wpar] = dm
                if what == "k":
                    de = o("ve", "tensor_copy", s["f6"][:, oc, :],
                           self.pb(bi, TT), deps=[dm])
                    dk.append(de)
                elif what == "v":
                    de = o("ve", "tensor_copy", s["f7"][:, oc, :],
                           self.pb(bi, TT), deps=[dm])
                    dv.append(de)
                else:
                    de = o("se", "activation", s["b11"][:, oc, :],
                           self.pb(bi, TT), AF.Sigmoid, deps=[dm])
                    dsr.append(de)
                self.bank_rd[bi] = de
        if TILE_STOP <= 4:
            return
        w = s["wkv"]
        drw = []
        for k in range(KC):
            kc = s["f6"][:, k, :]
            vc = s["f7"][:, k, :]
            d_ek = o("se", "activation", w[:, 0, :], kc, AF.Exp,
                     deps=[dk[k], p.dep("ve")])
            d_eku = o("se", "activation", w[:, 1, :], kc, AF.Exp,
                      bias=s["tfirst"][:, k:k + 1], deps=[dk[k]])
            d_ekv = o("ve", "tensor_tensor", w[:, 6, :], w[:, 0, :], vc,
                      ALU.mult, deps=[d_ek, dv[k]])
            d_ekuv = o("ve", "tensor_tensor", w[:, 7, :], w[:, 1, :], vc,
                       ALU.mult, deps=[d_eku, dv[k]])
            dA = o("ve", "tensor_tensor_scan", w[:, 2, :],
                   s["wdecT"][:, k * TT:(k + 1) * TT], w[:, 6, :],
                   s["carry"][:, k:k + 1], ALU.mult, ALU.add,
                   deps=[d_ekv, self.cvec_dep[-1]])
            dB = o("ve", "tensor_tensor_scan", w[:, 3, :],
                   s["wdecT"][:, k * TT:(k + 1) * TT], w[:, 0, :],
                   s["carry"][:, KC + k:KC + k + 1], ALU.mult, ALU.add,
                   deps=[dA])
            dn = o("ve", "tensor_tensor", w[:, 4, 1:TT], w[:, 2, 0:TT - 1],
                   w[:, 7, 1:TT], ALU.add, deps=[dA, d_ekuv])
            dn0 = o("ve", "tensor_scalar_add", w[:, 4, 0:1], w[:, 7, 0:1],
                    s["carry"][:, k:k + 1], deps=[d_ekuv])
            dd = o("ve", "tensor_tensor", w[:, 5, 1:TT], w[:, 3, 0:TT - 1],
                   w[:, 1, 1:TT], ALU.add, deps=[dB])
            dd0 = o("ve", "tensor_scalar_add", w[:, 5, 0:1], w[:, 1, 0:1],
                    s["carry"][:, KC + k:KC + k + 1], deps=[dB])
            dc1 = o("ve", "tensor_copy", s["carry"][:, k:k + 1],
                    w[:, 2, TT - 1:TT], deps=[dn, dn0])
            dc2 = o("ve", "tensor_copy", s["carry"][:, KC + k:KC + k + 1],
                    w[:, 3, TT - 1:TT], deps=[dd, dd0])
            drc = o("ve", "reciprocal", w[:, 5, :], w[:, 5, :],
                    deps=[dd, dd0])
            dy = o("ve", "tensor_tensor", w[:, 4, :], w[:, 4, :], w[:, 5, :],
                   ALU.mult, deps=[drc, dn, dn0])
            dr2 = o("ve", "tensor_tensor", s["b4"][:, k, :], s["b11"][:, k, :],
                    w[:, 4, :], ALU.mult, deps=[dy, dsr[k]])
            drw.append(dr2)
        if TILE_STOP <= 5:
            return
        dxp, dxpb, dxpsq = [], [], []
        for oc in range(KC):
            wpar0, wd0 = self._stripe("wshortB", "ws0", oc)
            wpar1, wd1 = self._stripe("woB", "ws1", oc)
            bi, brd = self._bank()
            dm = None
            for k in range(KC):
                dm = o("te", "matmul", self.pb(bi, TT),
                       s["ws0"][:, wpar0, k, :], s["b0"][:, k, :],
                       start=(k == 0), stop=False,
                       deps=[wd0, brd, dxn[k][1]] if k == 0 else [dxn[k][1]],
                       inc=False)
            for k in range(KC):
                dm = o("te", "matmul", self.pb(bi, TT),
                       s["ws1"][:, wpar1, k, :], s["b4"][:, k, :],
                       start=False, stop=(k == KC - 1),
                       deps=[wd1, drw[k]], inc=(k == KC - 1))
            self.ws_rd["ws0"][wpar0] = dm
            self.ws_rd["ws1"][wpar1] = dm
            d1 = o("ve", "tensor_copy", s["f5"][:, oc, :], self.pb(bi, TT),
                   deps=[dm])
            d2 = o("ve", "tensor_copy", s["b5"][:, oc, :], s["f5"][:, oc, :],
                   deps=[d1])
            dxp.append(d1)
            dxpb.append(d2)
            d3 = o("ve", "tensor_tensor", s["b6"][:, oc, :],
                   s["b5"][:, oc, :], s["b5"][:, oc, :], ALU.mult,
                   deps=[d2])
            dxpsq.append(d3)
            self.bank_rd[bi] = d1
        if TILE_STOP <= 6:
            return
        dsum = None
        for k in range(KC):
            dsum = o("te", "matmul", self.sb["psA"][0:1, 6 * 512:6 * 512 + TT],
                     s["onec_bf"][:], s["b5"][:, k, :],
                     start=(k == 0), stop=(k == KC - 1),
                     deps=([dxpb[k], self.bc6_rd, self.const_dep[-1]] + list(self.const_all)
                           if k == 0 else [dxpb[k]]),
                     inc=(k == KC - 1))
        dssq = None
        for k in range(KC):
            dssq = o("te", "matmul",
                     self.sb["psA"][0:1, 6 * 512 + TT:6 * 512 + 2 * TT],
                     s["onec_bf"][:], s["b6"][:, k, :],
                     start=(k == 0), stop=(k == KC - 1),
                     deps=[dxpsq[k]], inc=(k == KC - 1))
        psum_row = self.sb["psA"][0:1, 6 * 512:6 * 512 + TT]
        pssq_row = self.sb["psA"][0:1, 6 * 512 + TT:6 * 512 + 2 * TT]
        d1 = o("ve", "tensor_scalar_mul", s["rowB"][:], psum_row, 1.0 / C,
               deps=[dsum, dssq])
        d2 = o("ve", "tensor_scalar_mul", s["rowC"][:], pssq_row, 1.0 / C,
               deps=[dssq])
        d3 = o("ve", "tensor_tensor", s["rowD"][:], s["rowB"][:], s["rowB"][:],
               ALU.mult, deps=[d1])
        d3 = o("ve", "tensor_tensor", s["rowC"][:], s["rowC"][:], s["rowD"][:],
               ALU.subtract, deps=[d2, d3])
        d4 = o("se", "activation", s["rowC"][:], s["rowC"][:], AF.Ln,
               bias=s["kc"][0:1, 1:2], deps=[d3])
        d4 = o("se", "activation", s["rowC"][:], s["rowC"][:], AF.Exp,
               scale=-0.5, deps=[d4])
        d5 = o("ve", "tensor_tensor", s["rowB"][:], s["rowB"][:], s["rowC"][:],
               ALU.mult, deps=[d4])
        self.bc6_rd = d5
        db1 = o("te", "matmul", self.pb(4, TT), s["ones_f"][:], s["rowC"][:],
                start=True, stop=True, deps=[d4, self.bc4_rd])
        db2 = o("te", "matmul",
                self.sb["psA"][:, 4 * 512 + TT:4 * 512 + 2 * TT],
                s["ones_f"][:], s["rowB"][:], start=True, stop=True,
                deps=[d5])
        self.rowA_rd = db2
        dgc = []
        for k in range(KC):
            dgk = o("ve", "tensor_tensor", s["f4"][:, k, :], s["f5"][:, k, :],
                    self.pb(4, TT), ALU.mult, deps=[dxp[k], db1, db2])
            dgk = o("ve", "tensor_tensor", s["f4"][:, k, :], s["f4"][:, k, :],
                    self.sb["psA"][:, 4 * 512 + TT:4 * 512 + 2 * TT],
                    ALU.subtract, deps=[dgk, db2])
            dgk = o("ve", "tensor_scalar", s["f4"][:, k, :], s["f4"][:, k, :],
                    s["ln2w"][:, k:k + 1], s["ln2b"][:, k:k + 1],
                    ALU.mult, ALU.add, deps=[dgk])
            dgc.append(dgk)
        self.bc4_rd = dgc[-1]
        dgmix = {"gk": [], "gr": []}
        for nm, mixn, dst in [("gk", "fmixk", "b7"), ("gr", "fmixr", "b8")]:
            for k in range(KC):
                dm1 = o("ve", "tensor_scalar_mul", s["f6"][:, k, 1:TT],
                        s["f4"][:, k, 1:TT], s[mixn][:, k:k + 1],
                        deps=[dgc[k]])
                dm1 = o("ve", "scalar_tensor_tensor", s[dst][:, k, 1:TT],
                        s["f4"][:, k, 0:TT - 1], s[mixn + "1"][:, k:k + 1],
                        s["f6"][:, k, 1:TT], ALU.mult, ALU.add, deps=[dm1])
                dm2 = o("ve", "tensor_scalar_mul", s["f6"][:, k, 0:1],
                        s["f4"][:, k, 0:1], s[mixn][:, k:k + 1], deps=[dgc[k]])
                dm2 = o("ve", "scalar_tensor_tensor", s[dst][:, k, 0:1],
                        s["gcar"][:, k:k + 1], s[mixn + "1"][:, k:k + 1],
                        s["f6"][:, k, 0:1], ALU.mult, ALU.add, deps=[dm2])
                dgmix[nm].append((dm1, dm2))
        dgc2 = o("ve", "tensor_copy", s["gcar"][:], s["f4"][:, :, TT - 1],
                 deps=[d for pr in dgmix.values() for dd in pr for d in dd])
        if TILE_STOP <= 7:
            return
        dkk = []
        for oc in range(KF):
            wpar, wd = self._stripe("wffnkB", "ws0", oc)
            bi, brd = self._bank()
            dm = None
            for k in range(KC):
                dm = o("te", "matmul", self.pb(bi, TT),
                       s["ws0"][:, wpar, k, :], s["b7"][:, k, :],
                       start=(k == 0), stop=(k == KC - 1),
                       deps=([wd, brd, dgmix["gk"][k][0], dgmix["gk"][k][1]]
                             if k == 0
                             else [dgmix["gk"][k][0], dgmix["gk"][k][1]]),
                       inc=(k == KC - 1))
            self.ws_rd["ws0"][wpar] = dm
            d1 = o("ve", "tensor_scalar_max", s["b11"][:, oc % KC, :],
                   self.pb(bi, TT), 0.0, deps=[dm])
            d2 = o("ve", "tensor_tensor", s["kkbf"][:, oc, :],
                   self.pb(bi, TT), s["b11"][:, oc % KC, :], ALU.mult,
                   deps=[d1])
            self.bank_rd[bi] = d2
            dkk.append(d2)
        if TILE_STOP <= 8:
            return
        dsig = []
        for oc in range(KC):
            wpar, wd = self._stripe("wffnrB", "ws1", oc)
            bi, brd = self._bank()
            dm = None
            for k in range(KC):
                dm = o("te", "matmul", self.pb(bi, TT),
                       s["ws1"][:, wpar, k, :], s["b8"][:, k, :],
                       start=(k == 0), stop=(k == KC - 1),
                       deps=([wd, brd, dgmix["gr"][k][0], dgmix["gr"][k][1],
                              dkk[-1]]
                             if k == 0
                             else [dgmix["gr"][k][0], dgmix["gr"][k][1]]),
                       inc=(k == KC - 1))
            self.ws_rd["ws1"][wpar] = dm
            de = o("se", "activation", s["b11"][:, oc, :], self.pb(bi, TT),
                   AF.Sigmoid, deps=[dm, dkk[-1]])
            self.bank_rd[bi] = de
            dsig.append(de)
        if TILE_STOP <= 9:
            return
        dx1 = []
        for oc in range(KC):
            wpar, wd = self._stripe("wffnvB", "ws2", oc, nk=KF)
            bi, brd = self._bank()
            dm = None
            for k in range(KF):
                dm = o("te", "matmul", self.pb(bi, TT),
                       s["ws2"][:, wpar, k, :], s["kkbf"][:, k, :],
                       start=(k == 0), stop=(k == KF - 1),
                       deps=[wd, brd, dkk[k]] if k == 0 else [dkk[k]],
                       inc=(k == KF - 1))
            self.ws_rd["ws2"][wpar] = dm
            d1 = o("ve", "tensor_tensor", s["f7"][:, oc, :], self.pb(bi, TT),
                   s["b11"][:, oc, :], ALU.mult, deps=[dm, dsig[oc]])
            self.bank_rd[bi] = d1
            d2 = o("ve", "tensor_tensor", s["f7"][:, oc, :], s["f7"][:, oc, :],
                   s["f5"][:, oc, :], ALU.add, deps=[d1])
            d3 = o("ve", "tensor_tensor", s["f7"][:, oc, :], s["f7"][:, oc, :],
                   s["b10"][:, oc, :], ALU.mult, deps=[d2, dg0[oc]])
            d4 = o("ve", "tensor_tensor", s["f7"][:, oc, :], s["f7"][:, oc, :],
                   x0[:, oc, :], ALU.add, deps=[d3])
            dx1.append(d4)
        dst = self.dO.start(xout[b, :, :, tok0:tok0 + TT], s["f7"][:],
                            deps=dx1)
        self.x0_rd[par] = dx1[-1]
        self.f7_st = dst
        self.te_tail = dm
        self.last_store = dst



    def emit_l1_passA(self, b):
        g, s, p, o = self.io, self.sb, self.p, self.op
        xin = g["x1"].rearrange("b (k p) t -> b p k t", p=128)
        if not hasattr(self, "l1_init"):
            self.l1_init = True
            self.x0_rd = getattr(self, "x0_rd", {0: None, 1: None})
            self.x0_ld = getattr(self, "x0_ld", {0: None, 1: None})
            self.ws_rd = getattr(self, "ws_rd",
                                 {"ws0": {0: None, 1: None},
                                  "ws1": {0: None, 1: None},
                                  "ws2": {0: None, 1: None}})
            self.wsi = getattr(self, "wsi", {"ws0": 0, "ws1": 0, "ws2": 0})
            self.bank_rd = getattr(self, "bank_rd",
                                   {0: None, 1: None, 2: None, 3: None})
            self.banki = getattr(self, "banki", 0)
            self.bc4_rd = getattr(self, "bc4_rd", None)
            self.bc5_rd = None
            self.bc6_rd = getattr(self, "bc6_rd", None)
            self.x0_ld[0] = self.dX.start(
                s["f0"][:], xin[0, :, :, 0:TT],
                deps=[p.dep("ve"), self.x0_rd[0]])
        dz = [o("ve", "memset", s["xph"][:, :, 0:3], 0.0)]
        self.hm_first = True
        for t in range(self.NTT):
            self.emit_l1a_tile(b, t)
        dprev = None
        for kd in range(KD):
            trw = self.pbx(5, 0, 128, part=NST)
            dt = o("te", "transpose", trw,
                   s["hm"][:, kd * NST:(kd + 1) * NST], s["ident_f"][:],
                   deps=[self.hm_dep, dprev, self.bc5_rd])
            dc = o("ve", "tensor_scalar_mul",
                   s["hmT"][0:NST, kd * 128:(kd + 1) * 128],
                   trw, 1.0 / self.T, deps=[dt])
            dprev = dc
        self.bc5_rd = dprev
        self.hmT_dep = dprev

    def emit_l1a_tile(self, b, t):
        g, s, p, o = self.io, self.sb, self.p, self.op
        xin = g["x1"].rearrange("b (k p) t -> b p k t", p=128)
        par = (b * self.NTT + t) % 2
        x0 = s["f0"] if par == 0 else s["f1"]
        tok0 = t * TT
        xld = self.x0_ld[par]
        nb, nt = (b, t + 1) if t + 1 < self.NTT else (b + 1, 0)
        if nb < self.bc:
            np_ = 1 - par
            self.x0_ld[np_] = self.dX.start(
                (s["f0"] if np_ == 0 else s["f1"])[:],
                xin[nb, :, :, nt * TT:(nt + 1) * TT],
                deps=[self.x0_rd[np_]])
        tail = getattr(self, "te_tail", None)
        dxb, dxsq = [], []
        for k in range(KC):
            d1 = o("ve", "tensor_copy", s["b0"][:, k, :], x0[:, k, :],
                   deps=[xld, tail])
            d2 = o("ve", "tensor_tensor", s["b1"][:, k, :], s["b0"][:, k, :],
                   s["b0"][:, k, :], ALU.mult, deps=[d1])
            dxb.append(d1)
            dxsq.append(d2)
        dsum = None
        for k in range(KC):
            dsum = o("te", "matmul", self.sb["psA"][0:1, 6 * 512:6 * 512 + TT],
                     s["onec_bf"][:], s["b0"][:, k, :],
                     start=(k == 0), stop=(k == KC - 1),
                     deps=([dxb[k], self.bc6_rd, self.const_dep[-1]] + list(self.const_all)
                           if k == 0 else [dxb[k]]), inc=(k == KC - 1))
        dssq = None
        for k in range(KC):
            dssq = o("te", "matmul",
                     self.sb["psA"][0:1, 6 * 512 + TT:6 * 512 + 2 * TT],
                     s["onec_bf"][:], s["b1"][:, k, :],
                     start=(k == 0), stop=(k == KC - 1),
                     deps=[dxsq[k]], inc=(k == KC - 1))
        psum_row = self.sb["psA"][0:1, 6 * 512:6 * 512 + TT]
        pssq_row = self.sb["psA"][0:1, 6 * 512 + TT:6 * 512 + 2 * TT]
        d1 = o("ve", "tensor_scalar_mul", s["rowB"][:], psum_row, 1.0 / C,
               deps=[dsum, dssq])
        d2 = o("ve", "tensor_scalar_mul", s["rowC"][:], pssq_row, 1.0 / C,
               deps=[dssq])
        d3 = o("ve", "tensor_tensor", s["rowD"][:], s["rowB"][:], s["rowB"][:],
               ALU.mult, deps=[d1])
        d3 = o("ve", "tensor_tensor", s["rowC"][:], s["rowC"][:], s["rowD"][:],
               ALU.subtract, deps=[d2, d3])
        d4 = o("se", "activation", s["rowC"][:], s["rowC"][:], AF.Ln,
               bias=s["kc"][0:1, 1:2], deps=[d3])
        d4 = o("se", "activation", s["rowC"][:], s["rowC"][:], AF.Exp,
               scale=-0.5, deps=[d4])
        d5 = o("ve", "tensor_tensor", s["rowB"][:], s["rowB"][:], s["rowC"][:],
               ALU.mult, deps=[d4])
        self.bc6_rd = d5
        db1 = o("te", "matmul", self.pb(4, TT), s["ones_f"][:], s["rowC"][:],
                start=True, stop=True, deps=[d4, self.bc4_rd])
        db2 = o("te", "matmul",
                self.sb["psA"][:, 4 * 512 + TT:4 * 512 + 2 * TT],
                s["ones_f"][:], s["rowB"][:], start=True, stop=True, deps=[d5])
        dxn = []
        for k in range(KC):
            dk1 = o("ve", "tensor_tensor", s["f2"][:, k, :],
                    x0[:, k, :], self.pb(4, TT), ALU.mult,
                    deps=[xld, db1, db2])
            dk1 = o("ve", "tensor_tensor", s["b2"][:, k, :], s["f2"][:, k, :],
                    self.sb["psA"][:, 4 * 512 + TT:4 * 512 + 2 * TT],
                    ALU.subtract,
                    deps=[dk1, db2, getattr(self, "xn1_st", None)])
            dxn.append(dk1)
        self.bc4_rd = dxn[-1]
        self.x0_rd[par] = dxn[-1]
        dsp = self.dS.start(
            g["xn1Sp"].rearrange("b (k p) t -> b p k t", p=128)[b, :, :,
                                                               tok0:tok0 + TT],
            s["b2"][:], deps=dxn)
        self.xn1_st = dsp
        dgm = None
        for k in range(KC):
            dgm = o("te", "matmul", self.sb["psA"][0:NEXP, 5 * 512:5 * 512 + TT],
                    s["wgateS"][:, k, :], s["b2"][:, k, :],
                    start=(k == 0), stop=(k == KC - 1),
                    deps=([dxn[k], self.bc5_rd, self.wgate_dep]
                          if k == 0 else [dxn[k]]), inc=(k == KC - 1))
        dsc = o("se", "activation", s["scE"][:], 
                self.sb["psA"][0:NEXP, 5 * 512:5 * 512 + TT], AF.Sigmoid,
                deps=[dgm])
        dbi = o("ve", "tensor_scalar_add", s["biE"][:], s["scE"][:],
                s["gatebS"][:], deps=[dsc] + list(self.const_all) + list(self.const_all))
        self.bc5_rd = dbi
        for ls in range(TT // 128):
            csl = slice(ls * 128, (ls + 1) * 128)
            trv = self.pbx(5, 256, NEXP)
            dt1 = o("te", "transpose", trv, s["scE"][:, csl],
                    s["ident_f"][0:NEXP, 0:NEXP], deps=[dsc, dbi, self.bc5_rd])
            dt1c = o("ve", "tensor_copy", s["scrT"][:], trv, deps=[dt1])
            dt2 = o("te", "transpose", trv, s["biE"][:, csl],
                    s["ident_f"][0:NEXP, 0:NEXP], deps=[dbi, dt1c])
            dt2c = o("ve", "tensor_copy", s["biT"][:], trv, deps=[dt2])
            self.bc5_rd = dt2c
            dcp = o("ve", "tensor_copy", s["biT0"][:], s["biT"][:], deps=[dt2c])
            dd = dcp
            for r in range(3):
                dd = o("ve", "reduce_max", s["spc"][:, 0:1], s["biT"][:],
                       AX.X, deps=[dd])
                dd = o("ve", "tensor_scalar", s["mskT"][:], s["biT"][:],
                       s["spc"][:, 0:1], -1e30, ALU.is_ge, ALU.mult, deps=[dd])
                dd = o("ve", "tensor_tensor", s["biT"][:], s["biT"][:],
                       s["mskT"][:], ALU.add, deps=[dd])
            dd = o("ve", "reduce_max", s["spc"][:, 1:2], s["biT"][:], AX.X,
                   deps=[dd])
            dd = o("ve", "tensor_scalar", s["mskT"][:], s["biT0"][:],
                   s["spc"][:, 1:2], None, ALU.is_ge, deps=[dd])
            dd = o("ve", "tensor_tensor", s["scrT"][:], s["scrT"][:],
                   s["mskT"][:], ALU.mult, deps=[dd])
            dd = o("ve", "tensor_scalar", s["mskT"][:],
                   s["scrT"][:], 1.0, None, ALU.mult, ALU.add,
                   accum_out=s["spc"][:, 2:3], deps=[dd])
            dd = o("ve", "reciprocal", s["spc"][:, 3:4], s["spc"][:, 2:3],
                   deps=[dd])
            dd = o("ve", "tensor_scalar", s["scrT"][:], s["scrT"][:],
                   s["spc"][:, 3:4], 2.5, ALU.mult, ALU.mult, deps=[dd])
            trw = self.pbx(5, 256, 128, part=NEXP)
            dt3 = o("te", "transpose", trw, s["scrT"][:], s["ident_f"][:],
                    deps=[dd, self.bc5_rd])
            dt3c = o("ve", "tensor_copy", s["combS"][:], trw,
                     deps=[dt3, getattr(self, "comb_st", None)])
            self.bc5_rd = dt3c
            self.comb_st = self.dS.start(
                g["combSp"][b, :, tok0 + ls * 128:tok0 + (ls + 1) * 128],
                s["combS"][:], deps=[dt3c])
        dxp, dsz = [], []
        for oc in range(KF):
            wpar, wd = self._stripe("winB", "ws0", oc)
            bi, brd = self._bank()
            dm = None
            for k in range(KC):
                dm = o("te", "matmul", self.pb(bi, TT),
                       s["ws0"][:, wpar, k, :], s["b2"][:, k, :],
                       start=(k == 0), stop=(k == KC - 1),
                       deps=[wd, brd, dxn[k]] if k == 0 else [dxn[k]],
                       inc=(k == KC - 1))
            self.ws_rd["ws0"][wpar] = dm
            if oc < KD:
                de = o("ve", "tensor_copy", s["xph"][:, oc, 3:TT + 3],
                       self.pb(bi, TT), deps=[dm])
                dxp.append(de)
            else:
                de = o("se", "activation", s["silz"][:, oc - KD, :],
                       self.pb(bi, TT), AF.Silu,
                       deps=[dm, getattr(self, "silz_st", None)])
                dsz.append(de)
            self.bank_rd[bi] = de
        self.silz_st = dspz = self.dS.start(
            g["silzSp"].rearrange("b (k p) t -> b p k t", p=128)[b, :, :,
                                                                tok0:tok0 + TT],
            s["silz"][:], deps=dsz)
        dxc = []
        for kd in range(KD):
            acc = s["f2"][:, kd, :] if kd < KC else s["f3"][:, kd - KC, :]
            da = o("ve", "tensor_scalar_mul", acc,
                   s["xph"][:, kd, 0:TT], s["convw"][:, kd, 0:1],
                   deps=[dxp[kd], p.dep("te")] + list(self.const_all))
            for j in range(1, 4):
                da = o("ve", "scalar_tensor_tensor", acc,
                       s["xph"][:, kd, j:TT + j], s["convw"][:, kd, j:j + 1],
                       acc, ALU.mult, ALU.add, deps=[da])
            dc = o("se", "activation", s["xcb"][:, kd, :], acc,
                   AF.Silu, bias=s["convb"][:, kd:kd + 1],
                   deps=[da, getattr(self, "xc_st", None)])
            dh = o("ve", "tensor_copy", s["xph"][:, kd, 0:3],
                   s["xph"][:, kd, TT:TT + 3], deps=[da])
            dxc.append(dc)
        dspc = self.dS.start(
            g["xcSp"].rearrange("b (k p) t -> b p k t", p=128)[b, :, :,
                                                              tok0:tok0 + TT],
            s["xcb"][:], deps=dxc)
        self.xc_st = dspc
        dsm = None
        for kd in range(KD):
            dsm = o("te", "matmul", self.sb["psA"][0:33, 5 * 512:5 * 512 + TT],
                    s["wxpS"][:, kd, :], s["xcb"][:, kd, :],
                    start=(kd == 0), stop=(kd == KD - 1),
                    deps=([dxc[kd], self.bc5_rd, self.wxp_dep]
                          if kd == 0 else [dxc[kd]]), inc=(kd == KD - 1))
        dse = o("ve", "tensor_copy", s["ssmE"][:],
                self.sb["psA"][0:33, 5 * 512:5 * 512 + TT], deps=[dsm])
        self.bc5_rd = dse
        for ls in range(TT // 128):
            csl = slice(ls * 128, (ls + 1) * 128)
            trv = self.pbx(5, 256, 33)
            dt1 = o("te", "transpose", trv, s["ssmE"][:, csl],
                    s["ident_f"][0:33, 0:33], deps=[dse, self.bc5_rd])
            dt1c = o("ve", "tensor_copy", s["ssmT"][:, ls * 33:(ls + 1) * 33],
                     trv, deps=[dt1])
            self.bc5_rd = dt1c
            dcol = s["ssmT"][:, ls * 33:ls * 33 + 1]
            dsp1 = o("se", "activation", s["spc"][:, 4:5], dcol, AF.Abs,
                     deps=[dt1c])
            dsp1 = o("se", "activation", s["spc"][:, 4:5], s["spc"][:, 4:5],
                     AF.Exp, scale=-1.0, deps=[dsp1])
            dsp1 = o("se", "activation", s["spc"][:, 4:5], s["spc"][:, 4:5],
                     AF.Ln, bias=1.0, deps=[dsp1])
            dsp2 = o("ve", "tensor_scalar_max", s["spc"][:, 5:6], dcol, 0.0,
                     deps=[dt1c])
            dsp2 = o("ve", "tensor_tensor", s["spc"][:, 5:6], s["spc"][:, 5:6],
                     s["spc"][:, 4:5], ALU.add, deps=[dsp1, dsp2])
            ddb = o("ve", "tensor_scalar_mul", s["dBtm"][:, ls, :],
                    s["ssmT"][:, ls * 33 + 1:ls * 33 + 17], s["spc"][:, 5:6],
                    deps=[dsp2])
            trn = self.pbx(5, 256 + 64, 128, part=NST)
            dt2 = o("te", "transpose", trn,
                    s["ssmT"][:, ls * 33 + 17:(ls + 1) * 33],
                    s["ident_f"][:], deps=[dt1c, self.bc5_rd])
            dt2c = o("ve", "tensor_copy",
                     s["cmT"][:, tok0 + ls * 128:tok0 + (ls + 1) * 128],
                     trn, deps=[dt2])
            self.bc5_rd = dt2c
            dxl = self.dma_tr(
                s["xcl"][:],
                g["xcSp"][b, :, tok0 + ls * 128:tok0 + (ls + 1) * 128],
                deps=[dspc, getattr(self, "xcl_rd", None)])
            bi, brd = self._bank()
            dhm = None
            for kd in range(KD):
                dhm = o("te", "matmul",
                        self.pb(bi, TT)[:, kd * NST:(kd + 1) * NST],
                        s["xcl"][:, kd * 128:(kd + 1) * 128],
                        s["dBtm"][:, ls, :],
                        start=(kd == 0), stop=(kd == KD - 1),
                        deps=([dxl, ddb, brd] if kd == 0 else [ddb]),
                        inc=(kd == KD - 1))
            if self.hm_first:
                dac = o("ve", "tensor_copy", s["hm"][:],
                        self.pb(bi, TT)[:, 0:KD * NST], deps=[dhm])
                self.hm_first = False
            else:
                dac = o("ve", "tensor_tensor", s["hm"][:], s["hm"][:],
                        self.pb(bi, TT)[:, 0:KD * NST], ALU.add,
                        deps=[dhm, self.hm_dep])
            self.bank_rd[bi] = dac
            self.xcl_rd = dhm
            self.hm_dep = dac
        self.te_tail = dsm

    def dma_tr(self, out, in_, deps=()):
        n = self.dX.names[self.dX.i % len(self.dX.names)]
        self.dX.i += 1
        self.p.wait("sy", self.p.dep(n), *deps)
        inst = self.p.eng["sy"].dma_start(out=out, in_=in_, transpose=True)
        return self.p.inc(n, inst, 16)

    def dS_all(self):
        return tuple(self.p.dep(n) for n in self.dS.names)

    def emit_l1_passB(self, b):
        g, s, p, o = self.io, self.sb, self.p, self.op
        self.spill_all = self.dS_all()
        for t in range(self.NTT):
            self.emit_l1b_tile(b, t)
        self.emit_moe(b)
        xin = g["x1"].rearrange("b (k p) t -> b p k t", p=128)
        xout = g["out_fm"].rearrange("b (k p) t -> b p k t", p=128)
        s1in = g["s1Sp"].rearrange("b (k p) t -> b p k t", p=128)
        spill2 = self.dS_all()
        for t in range(self.NTT):
            tok0 = t * TT
            dx = self.dX.start(s["f0"][:], xin[b, :, :, tok0:tok0 + TT],
                               deps=[self.x0_rd[0], p.dep("ve")])
            ds1 = self.dX.start(s["b0"][:], s1in[b, :, :, tok0:tok0 + TT],
                                deps=list(spill2) +
                                [getattr(self, "b6_rd", None), p.dep("ve")])
            dg1 = o("se", "activation", s["b1"][:], s["yacc"][:, :, tok0:tok0 + TT],
                    AF.Silu, deps=[self.yacc_dep, getattr(self, "b7_rd", None)])
            dt1 = o("ve", "tensor_tensor", s["f3"][:], s["b0"][:], s["b1"][:],
                    ALU.mult,
                    deps=[ds1, dg1, getattr(self, "out_st", None)])
            dt2 = o("ve", "tensor_tensor", s["f3"][:], s["f3"][:], s["f0"][:],
                    ALU.add, deps=[dx, dt1])
            dst = self.dO.start(xout[b, :, :, tok0:tok0 + TT], s["f3"][:],
                                deps=[dt2])
            self.out_st = dst
            self.x0_rd[0] = dt2
            self.b6_rd = dt1
            self.b7_rd = dt1
            self.last_store = dst

    def emit_l1b_tile(self, b, t):
        g, s, p, o = self.io, self.sb, self.p, self.op
        tok0 = t * TT
        dxc = self.dX.start(
            s["xcb"][:],
            g["xcSp"].rearrange("b (k p) t -> b p k t", p=128)[b, :, :,
                                                              tok0:tok0 + TT],
            deps=list(self.spill_all) + [p.dep("te"), p.dep("ve"), p.dep("se")])
        dsz = self.dX.start(
            s["silz"][:],
            g["silzSp"].rearrange("b (k p) t -> b p k t", p=128)[b, :, :,
                                                                tok0:tok0 + TT],
            deps=[p.dep("ve")])
        ds1b = []
        for kd in range(KD):
            bi, brd = self._bank()
            dm = o("te", "matmul", self.pb(bi, TT),
                   s["hmT"][0:NST, kd * 128:(kd + 1) * 128],
                   s["cmT"][0:NST, tok0:tok0 + TT],
                   start=True, stop=True,
                   deps=[self.hmT_dep, brd])
            dy = o("ve", "scalar_tensor_tensor", s["f3"][:, 0, :],
                   s["xcb"][:, kd, :], s["dvec"][:, kd:kd + 1],
                   self.pb(bi, TT), ALU.mult, ALU.add,
                   deps=[dm, dxc] + list(self.const_all) + list(self.const_all))
            self.bank_rd[bi] = dy
            dst_sl = s["b3"][:, kd, :] if kd < KC else s["b4"][:, kd - KC, :]
            dsb = o("ve", "tensor_tensor", dst_sl, s["f3"][:, 0, :],
                    s["silz"][:, kd, :], ALU.mult, deps=[dy, dsz])
            ds1b.append(dsb)
        dsp = []
        for oc in range(KC):
            wpar, wd = self._stripe("woutB", "ws2", oc, nk=KD)
            bi, brd = self._bank()
            dm = None
            for k in range(KD):
                rhs = s["b3"][:, k, :] if k < KC else s["b4"][:, k - KC, :]
                dm = o("te", "matmul", self.pb(bi, TT),
                       s["ws2"][:, wpar, k, :], rhs,
                       start=(k == 0), stop=(k == KD - 1),
                       deps=[wd, brd, ds1b[k]] if k == 0 else [ds1b[k]],
                       inc=(k == KD - 1))
            self.ws_rd["ws2"][wpar] = dm
            de = o("ve", "tensor_copy", s["b5"][:, oc, :], self.pb(bi, TT),
                   deps=[dm, getattr(self, "s1_st", None)])
            self.bank_rd[bi] = de
            dsp.append(de)
        self.s1_st = self.dS.start(
            g["s1Sp"].rearrange("b (k p) t -> b p k t", p=128)[b, :, :,
                                                              tok0:tok0 + TT],
            s["b5"][:], deps=dsp)
        self.te_tail = dm

    def emit_moe(self, b):
        g, s, p, o = self.io, self.sb, self.p, self.op
        w13 = g["w13B"].rearrange("e (k p) j -> e p k j", p=128)
        w2 = g["w2B"].rearrange("e (k p) j -> e p k j", p=128)
        w13s = g["w13sB"].rearrange("(k p) j -> p k j", p=128)
        w2s = g["w2sB"].rearrange("(k p) j -> p k j", p=128)
        xn1 = g["xn1Sp"].rearrange("b (k p) t -> b p k t", p=128)
        nst_ = self.T // TT
        moe_w_rd = getattr(self, "moe_w_rd", None)
        xn1_rd = None
        yacc_dep = None
        first_g = True
        for grp in range(NEXP // 2 + 1):
            shared = grp == NEXP // 2
            nexp_g = 1 if shared else 2
            wld = []
            for j in range(nexp_g):
                if shared:
                    wld.append(self.dW.start(s["moeW13"][:, j], w13s,
                                             deps=[moe_w_rd] + list(self.cast_dep)))
                    wld.append(self.dW.start(s["moeW2"][:, j], w2s, deps=()))
                else:
                    e = 2 * grp + j
                    wld.append(self.dW.start(s["moeW13"][:, j], w13[e],
                                             deps=[moe_w_rd] + list(self.cast_dep)))
                    wld.append(self.dW.start(s["moeW2"][:, j], w2[e], deps=()))
            for st_ in range(nst_):
                tok0 = st_ * TT
                dxn = self.dX.start(s["xn1m"][:], xn1[b, :, :, tok0:tok0 + TT],
                                    deps=list(self.spill_all) + [xn1_rd])
                ymm = None
                first_y = True
                for j in range(nexp_g):
                    if not shared:
                        e = 2 * grp + j
                        dcr = self.dR.start(
                            s["rowA"][:], g["combSp"][b, e:e + 1, tok0:tok0 + TT],
                            deps=list(self.spill_all) +
                            [getattr(self, "rowA_rd", None)])
                        dcb = o("te", "matmul", self.pbx(4, 0, TT),
                                s["ones_f"][:], s["rowA"][:], start=True,
                                stop=True, deps=[dcr, getattr(self, "cb_rd", None)])
                        self.rowA_rd = dcb
                        dcbc = o("ve", "tensor_copy", s["cbt"][:],
                                 self.pbx(4, 0, TT),
                                 deps=[dcb, getattr(self, "cbt_rd", None)])
                        self.cb_rd = dcbc
                    for jp in range(KJ2):
                        gps = self.pbx(5, 0, TT)
                        ups = self.pbx(6, 0, TT)
                        dg = None
                        for k in range(KC):
                            dg = o("te", "matmul", gps,
                                   s["moeW13"][:, j, k, jp * 128:(jp + 1) * 128],
                                   s["xn1m"][:, k, :],
                                   start=(k == 0), stop=(k == KC - 1),
                                   deps=([dxn, wld[2 * j],
                                          getattr(self, "b5rd", None)]
                                         if k == 0 else []),
                                   inc=(k == KC - 1))
                        du = None
                        for k in range(KC):
                            du = o("te", "matmul", ups,
                                   s["moeW13"][:, j, k,
                                               I_MOE + jp * 128:I_MOE + (jp + 1) * 128],
                                   s["xn1m"][:, k, :],
                                   start=(k == 0), stop=(k == KC - 1),
                                   deps=([getattr(self, "up_rd", None)]
                                         if k == 0 else []),
                                   inc=(k == KC - 1))
                        dsg = o("se", "activation", s["silug"][:], gps,
                                AF.Silu, deps=[dg])
                        if shared:
                            dhp = o("ve", "tensor_tensor", s["hp"][:, jp, :],
                                    s["silug"][:], ups, ALU.mult,
                                    deps=[du, dsg])
                        else:
                            dhp = o("ve", "tensor_tensor", s["hp"][:, jp, :],
                                    ups, s["cbt"][:], ALU.mult,
                                    deps=[du, dcbc])
                            dhp = o("ve", "tensor_tensor", s["hp"][:, jp, :],
                                    s["hp"][:, jp, :], s["silug"][:],
                                    ALU.mult, deps=[dhp, dsg])
                        self.b5rd = dhp
                        self.up_rd = dhp
                        self.cbt_rd = dhp if not shared else getattr(self, "cbt_rd", None)
                        last_sweep = (j == nexp_g - 1 and jp == KJ2 - 1)
                        for oc in range(KC):
                            yv = self.sb["psA"][:, oc * TT:(oc + 1) * TT]
                            dps = [dhp]
                            if oc == 0:
                                dps.append(wld[2 * j + 1])
                            if first_y:
                                dps += [self.bank_rd[bb] for bb in range(4)]
                            ymm = o("te", "matmul", yv,
                                    s["moeW2"][:, j, jp, oc * 128:(oc + 1) * 128],
                                    s["hp"][:, jp, :],
                                    start=(first_y and oc % 2 == 0),
                                    stop=(last_sweep and oc % 2 == 1),
                                    deps=dps, inc=(oc == KC - 1))
                        first_y = False
                dev = None
                for oc in range(KC):
                    yv = self.sb["psA"][:, oc * TT:(oc + 1) * TT]
                    ysl = s["yacc"][:, oc, tok0:tok0 + TT]
                    if first_g:
                        dev = o("ve", "tensor_copy", ysl, yv, deps=[ymm])
                    else:
                        dev = o("ve", "tensor_tensor", ysl, ysl, yv, ALU.add,
                                deps=[ymm, yacc_dep])
                for bb in range(4):
                    self.bank_rd[bb] = dev
                xn1_rd = ymm
                moe_w_rd = ymm
                yacc_dep = dev
            first_g = False
        self.moe_w_rd = moe_w_rd
        self.yacc_dep = yacc_dep

    def emit_l1_batch(self, b):
        self.emit_l1_passA(b)
        self.emit_l1_passB(b)




def build_l0(bc, t_full, n_stats_rows, stop=99, nbat=None):
    B = Build("l0", bc, t_full, n_stats_rows)
    B.declare_io()
    B.alloc()
    B.emit_startup()
    if stop >= 2:
        B.emit_wquant()
    if stop >= 3:
        B.emit_stats()
    if stop >= 4:
        B.emit_scales()
    if stop >= 5:
        for b in range(bc if nbat is None else nbat):
            B.emit_l0_batch(b)
    p = B.p
    for ring in (B.dO, B.dS, B.dX, B.dW, B.dR, B.dC):
        p.wait("sy", *[p.dep(n) for n in ring.names])
    nc = B.finish()
    return nc


def build_l1(bc, t_full):
    B = Build("l1", bc, t_full, 0)
    B.declare_io()
    B.alloc()
    B.emit_startup()
    for b in range(bc):
        B.emit_l1_batch(b)
    p = B.p
    for ring in (B.dO, B.dS, B.dX, B.dW, B.dR, B.dC):
        p.wait("sy", *[p.dep(n) for n in ring.names])
    nc = B.finish()
    return nc


_L0_SHARED = ["wk", "wv", "wr", "wo", "wshort", "wffnk", "wffnr", "wffnv",
              "wl0", "ln1w", "ln1b", "ln2w", "ln2b", "tdec", "tfirst",
              "mixk", "mixv", "mixr", "fmixk", "fmixr", "l0bias",
              "ident_bf", "ident_f", "ones_row_f", "ones_col_bf"]
_L1_SHARED = ["win", "wxproj", "wout", "wgate", "w13", "w2", "w13s", "w2s",
              "convw", "convb", "dvec", "gateb",
              "ident_bf", "ident_f", "ones_row_f", "ones_col_bf"]


def kernel(**inputs):
    g, percore = prep_host(inputs)
    bc = B_TOT // N_CORES
    cores = list(range(N_CORES))

    nc0 = build_l0(bc, T_FULL, B_TOT * T_FULL)
    sh0 = {k: g[k] for k in _L0_SHARED}
    im0 = [{**sh0, "x_fm": percore[c]["x_fm"], "x_tm": percore[c]["x_tm"]}
           for c in range(N_CORES)]
    res0 = run_bass_kernel_spmd(nc0, im0, core_ids=cores)
    x1s = [np.asarray(r["x1"]) for r in res0.results]

    nc1 = build_l1(bc, T_FULL)
    sh1 = {k: g[k] for k in _L1_SHARED}
    im1 = [{**sh1, "x1": x1s[c]} for c in range(N_CORES)]
    res1 = run_bass_kernel_spmd(nc1, im1, core_ids=cores)
    out_fm = np.concatenate([np.asarray(r["out_fm"]) for r in res1.results],
                            axis=0)
    return np.ascontiguousarray(out_fm.transpose(0, 2, 1))
